# version 17
# baseline (speedup 1.0000x reference)
import numpy as np
import ml_dtypes
import concourse.bass as bass
import concourse.mybir as mybir
from concourse.bass_utils import run_bass_kernel_spmd

F32 = mybir.dt.float32
BF16 = mybir.dt.bfloat16
AF = mybir.ActivationFunctionType
ALU = mybir.AluOpType
AX = mybir.AxisListType
BFNP = ml_dtypes.bfloat16

C = 1024
NCH = 8
H = 16
HD = 64
HP = HD + 2
EPS = 1e-5
NCORE = 8

ENG_SEM = {"t": "sT", "v": "sV", "s": "sS", "g": "sG", "y": "sY"}

SEMS = {}


class PB:

    def __init__(self):
        self.q = {e: [] for e in ENG_SEM}
        self.cnt = {}
        self.waited = {e: {} for e in ENG_SEM}
        self.last_dma = {}

    def val(self, sem):
        return self.cnt.get(sem, 0)

    def wait(self, eng, sem, v):
        if v is None or v <= 0:
            return
        assert v <= self.val(sem), f"wait on future: {eng} {sem} {v} > {self.val(sem)}"
        if self.waited[eng].get(sem, 0) >= v:
            return
        self.waited[eng][sem] = v
        self.q[eng].append(lambda e, _s=sem, _v=v: e.wait_ge(SEMS[_s], _v))

    def op(self, eng, emit, deps=(), dma=None):
        deps = [d for d in deps if d is not None]
        if dma:
            sem = dma if isinstance(dma, str) else (
                "sGD" if eng == "g" else "sY")
            prev = self.last_dma.get(sem)
            if prev is not None:
                deps.append(prev)
            by = 16
        else:
            sem = ENG_SEM[eng]
            by = 1
        for (s, v) in deps:
            self.wait(eng, s, v)
        v = self.cnt[sem] = self.val(sem) + by
        if dma:
            self.last_dma[sem] = (sem, v)
        self.q[eng].append(lambda e, _f=emit, _s=sem, _by=by: _f(e).then_inc(SEMS[_s], _by))
        return (sem, v)


def build_nc(NT, QTOT):
    global SEMS
    TL = NT * 128
    NB = max(1, TL // 512)
    BLK = TL // NB
    TPB = NT // NB

    nc = bass.Bass("TRN2", target_bir_lowering=False, debug=False,
                   dynamic_dma_scratch_size=8192)

    D = lambda n, sh, dt=F32, kind="ExternalInput": nc.dram_tensor(n, sh, dt, kind=kind).ap()
    x_in = D("x_in", [TL, C])
    wkvT = D("wkvT", [C, 2 * C])
    wqT = D("wqT", [C, C])
    p1T = D("p1T", [C, C])
    p2T = D("p2T", [C, C])
    wT = D("wT", [C, C])
    vrW = D("vrW", [C, 32])
    ctp = D("ctp", [128, C])
    qb_pp = D("qb_pp", [128, NCH])
    p1b_pp = D("p1b_pp", [128, NCH])
    kvb_pad = D("kvb_pad", [1, 2 * C])
    p2b_pad = D("p2b_pad", [1, C])
    fb_pad = D("fb_pad", [1, C])
    kng_pad = D("kng_pad", [128, HD])
    vng_pad = D("vng_pad", [128, HD])
    vnb_pad = D("vnb_pad", [128, HD])
    vng66 = D("vng66", [33, HD])
    vnb66 = D("vnb66", [33, HD])
    kn2_66 = D("kn2_66", [128, HD])
    tempb = D("tempb", [HD, H])
    abt = D("abt", [HD, H * HD])
    eye_d = D("eye_d", [128, 128], BF16)
    epad_d = D("epad_d", [128, 128], BF16)
    out_d = D("out", [TL, C], F32, kind="ExternalOutput")
    cc_in = D("cc_in", [HP, H * HP], F32, kind="Internal")
    cc_out = D("cc_out", [HP, H * HP], F32, kind="Internal")

    import contextlib
    es = contextlib.ExitStack()
    with es:
        SB = lambda n, sh, dt=F32: es.enter_context(nc.sbuf_tensor(n, sh, dt))
        PS = lambda n, sh, dt=F32: es.enter_context(nc.psum_tensor(n, sh, dt))

        x1t = SB("x1t", [128, NCH, TL], BF16)
        wa = SB("wa", [128, NCH, 2048], BF16)
        wt_sb = SB("wt_sb", [128, NCH, C], BF16)
        vr_sb = SB("vr_sb", [128, NCH, 32], BF16)
        ct_sb = SB("ct_sb", [128, C], BF16)
        qb_sb = SB("qb_sb", [128, NCH])
        p1b_sb = SB("p1b_sb", [128, NCH])
        kvb_sb = SB("kvb_sb", [1, 2 * C], BF16)
        p2b_sb = SB("p2b_sb", [1, C], BF16)
        fb_sb = SB("fb_sb", [1, C], BF16)
        kngp_sb = SB("kngp_sb", [128, HD])
        vngp_sb = SB("vngp_sb", [128, HD])
        vnbp_sb = SB("vnbp_sb", [128, HD])
        vng66_sb = SB("vng66_sb", [33, HD])
        c4vb66 = SB("c4vb66", [33, HD])
        kn2_sb = SB("kn2_sb", [128, HD], BF16)
        rows66 = SB("rows66", [128, H * HD], BF16)
        rscr66 = SB("rscr66", [33, HD])
        rscr66b = SB("rscr66b", [33, HD])
        tempq = SB("tempq", [HD, H])
        ab_sb = SB("ab_sb", [HD, H * HD])
        eye_sb = SB("eye_sb", [128, 128], BF16)
        epad_sb = SB("epad_sb", [128, 128], BF16)
        G2_sb = SB("G2_sb", [HD, HD])
        T1_sb = SB("T1_sb", [HD, HD])
        kvf_all = SB("kvf_all", [HD, H * HD], BF16)
        kvf2 = SB("kvf2", [128, NCH * 128], BF16)
        ccs = SB("ccs", [HP, H * HP])
        kvsum = SB("kvsum", [HP, H * HP])
        r1 = SB("r1", [128, NT])
        b1 = SB("b1", [128, NT])
        xr = SB("xr", [128, 1, C])
        x2b_t = SB("x2b_t", [128, 3, C], BF16)
        kraw = SB("kraw", [128, C])
        vraw = SB("vraw", [128, C])
        scr = SB("scr", [128, C])
        uk = SB("uk", [128, 2, H * HP], BF16)
        uv = SB("uv", [128, 2, H * HP], BF16)
        st_s = SB("st_s", [128, H])
        st_q = SB("st_q", [128, H])
        st_m = SB("st_m", [128, H])
        st_v = SB("st_v", [128, H])
        st_r = SB("st_r", [128, H])
        mv2 = SB("mv2", [128, 2])
        bnst = SB("bnst", [128, 12])
        sd_c = SB("sd_c", [128, 1])
        rc_c = SB("rc_c", [128, 1])
        mrc_c = SB("mrc_c", [128, 1])
        eps_c = SB("eps_c", [128, 1])
        qbl = SB("qbl", [128, NB, NCH, BLK], BF16)
        attp = SB("attp", [128, NCH, BLK], BF16)
        hb = SB("hb", [128, NCH, BLK], BF16)
        xv_sb = SB("xv_sb", [128, 128], BF16)
        zb = SB("zb", [128, C])
        yb = SB("yb", [128, C])
        xpb = SB("xpb", [128, 3, C])
        e1 = SB("e1", [HD, HD])
        e2 = SB("e2", [HD, HD])
        xob = SB("xob", [128, C])
        NQ = NB

        pA = PS("pA", [128, 512]); pB = PS("pB", [128, 512])
        pC = PS("pC", [128, 512]); pD = PS("pD", [128, 512])
        pE = PS("pE", [128, 512]); pF = PS("pF", [128, 512])
        pG = PS("pG", [128, 512])
        pT = PS("pT", [128, 1024], BF16)

        SEMS = {n: es.enter_context(nc.semaphore(n)) for n in
                ("sT", "sV", "sS", "sG", "sGD", "sY", "sYx", "sYo", "sYk", "sC")}

        P = PB()
        T = lambda emit, deps=(): P.op("t", emit, deps)
        V = lambda emit, deps=(): P.op("v", emit, deps)
        S = lambda emit, deps=(): P.op("s", emit, deps)
        G = lambda emit, deps=(): P.op("g", emit, deps)
        YD = lambda emit, deps=(): P.op("y", emit, deps, dma="sY")
        YX = lambda emit, deps=(): P.op("y", emit, deps, dma="sYx")
        YO = lambda emit, deps=(): P.op("y", emit, deps, dma="sYo")
        YK = lambda emit, deps=(): P.op("y", emit, deps, dma="sYk")
        GD = lambda emit, deps=(): P.op("g", emit, deps, dma="sGD")

        for dst, src in ((qb_sb, qb_pp), (p1b_sb, p1b_pp), (kngp_sb, kng_pad),
                         (vngp_sb, vng_pad), (vnbp_sb, vnb_pad), (vng66_sb, vng66),
                         (c4vb66, vnb66), (tempq, tempb), (ab_sb, abt),
                         (eye_sb, eye_d), (epad_sb, epad_d)):
            smalls_y = YD(lambda e, d=dst, s=src: e.dma_start(out=d[:], in_=s[:]))
        for dst, src in ((kvb_sb, kvb_pad), (p2b_sb, p2b_pad), (fb_sb, fb_pad),
                         (kn2_sb, kn2_66), (ct_sb, ctp)):
            smalls_g = GD(lambda e, d=dst, s=src: e.dma_start(out=d[:], in_=s[:]))
        smalls_g = GD(lambda e: e.dma_start(
            out=vr_sb[:], in_=vrW.rearrange("(c p) r -> p c r", p=128)))
        wkv_lds = []
        for c in range(NCH):
            wkv_lds.append(GD(lambda e, c=c: e.dma_start(
                out=wa[:, c, :], in_=wkvT[c * 128:(c + 1) * 128, :])))
        wq_ld = GD(lambda e: e.dma_start(
            out=wt_sb[:], in_=wqT.rearrange("(c p) n -> p c n", p=128)))

        ms = []
        for fn in (lambda e: e.memset(uk[:], 0.0),
                   lambda e: e.memset(uv[:], 0.0),
                   lambda e: e.memset(
                       uk[:].rearrange("p b (h c) -> p b h c", c=HP)
                       [:, :, :, HD:HD + 1], 1.0),
                   lambda e: e.memset(
                       uv[:].rearrange("p b (h c) -> p b h c", c=HP)
                       [:, :, :, HD:HD + 1], 1.0),
                   lambda e: e.memset(kvf2[:], 0.0),
                   lambda e: e.memset(rows66[:], 0.0),
                   lambda e: e.memset(xv_sb[:], 0.0),
                   lambda e: e.memset(eps_c[:], EPS)):
            ms.append(V(fn, deps=ms[-1:]))
        v_tempq = V(lambda e: e.tensor_scalar(tempq[:], tempq[:], 1.0 / QTOT, None,
                                              ALU.mult), deps=[smalls_y])
        v_c4vb = V(lambda e: e.tensor_scalar(c4vb66[0:1, :], c4vb66[0:1, :],
                                             float(QTOT), None, ALU.mult),
                   deps=[smalls_y])
        v_kn2n = V(lambda e: e.tensor_scalar(kn2_sb[32:33, :], kn2_sb[32:33, :],
                                             -1.0, None, ALU.mult), deps=[smalls_g])
        t_g2 = T(lambda e: e.matmul(pG[0:HD, 0:HD], kngp_sb[:], vngp_sb[:],
                                    start=True, stop=True), deps=[smalls_y])
        t_t1 = T(lambda e: e.matmul(pG[0:HD, 64:128], kngp_sb[:], vnbp_sb[:],
                                    start=True, stop=True), deps=[smalls_y])
        v_g2 = V(lambda e: e.tensor_copy(G2_sb[:], pG[0:HD, 0:HD]), deps=[t_g2, t_t1])
        v_t1 = V(lambda e: e.tensor_copy(T1_sb[:], pG[0:HD, 64:128]), deps=[t_g2, t_t1])

        pbank = [pA, pB, pC, pD]
        pbank_war = [None] * 4
        u_war = [None, None]
        kraw_war = None
        vraw_war = None
        scr_war = None
        kv_first = {0: True, 1: True, 2: True}
        last_kmm = None
        last_vmm = None
        kvmm_last = None
        kv_banks = {0: pE, 1: pF, 2: pG}
        pending_kv = None

        def emit_kv(i, uk_ready, uv_ready, uk_r2, uv_r2):
            nonlocal kvmm_last
            slot = i % 2
            for h in range(H):
                bk = 0 if h < 7 else (1 if h < 14 else 2)
                col = (h - (0, 7, 14)[bk]) * HP
                first = kv_first[bk] and i == 0
                dep_extra = [v_g2, v_t1] if (first and bk == 2) else []
                if first:
                    kv_first[bk] = False
                lastmm = (i == NT - 1) and (h in (6, 13, 15))
                kvmm_last = T(lambda e, h=h, bk=bk, col=col, slot=slot, first=first,
                              lastmm=lastmm: e.matmul(
                    kv_banks[bk][0:HP, col:col + HP],
                    uk[:, slot, h * HP:(h + 1) * HP],
                    uv[:, slot, h * HP:(h + 1) * HP],
                    start=first, stop=lastmm,
                    skip_group_check=not (first or lastmm)),
                    deps=[uk_ready, uv_ready, uk_r2, uv_r2] + dep_extra)
            u_war[slot] = kvmm_last

        x1b_war = None
        pT_war = None
        xr_war = [None, None]
        x1t_rdy = []
        r1b1_v = None
        last_sqrt = None

        def emit_P1(i):
            nonlocal x1b_war, pT_war, r1b1_v, last_sqrt
            s = i % 2
            xt, xs = (xr, 0) if s == 0 else (xpb, 1)
            ldx = YX(lambda e, i=i, xt=xt, xs=xs: e.dma_start(
                out=xt[:, xs, :], in_=x_in[i * 128:(i + 1) * 128, :]),
                deps=[xr_war[s]])
            b1a = V(lambda e, xt=xt, xs=xs: e.bn_stats(
                bnst[:, 0:6], xt[:, xs, 0:512]), deps=[ldx, last_sqrt])
            b1b = V(lambda e, xt=xt, xs=xs: e.bn_stats(
                bnst[:, 6:12], xt[:, xs, 512:1024]), deps=[b1a])
            ag = V(lambda e: e.bn_aggr(mv2[:], bnst[:]), deps=[b1a, b1b])
            sq = S(lambda e: e.activation(sd_c[:], mv2[:, 1:2], AF.Sqrt, bias=eps_c[:]),
                   deps=[ag])
            last_sqrt = sq
            rc = V(lambda e: e.reciprocal(rc_c[:], sd_c[:]), deps=[sq])
            r1w = V(lambda e, i=i: e.tensor_copy(r1[:, i:i + 1], rc_c[:]), deps=[rc])
            b1w = V(lambda e, i=i: e.tensor_scalar(b1[:, i:i + 1], mv2[:, 0:1],
                                                   rc_c[:], -1.0, ALU.mult, ALU.mult),
                    deps=[rc])
            r1b1_v = b1w
            xb = V(lambda e, i=i, xt=xt, xs=xs: e.tensor_scalar(
                x2b_t[:, 0, :], xt[:, xs, :],
                r1[:, i:i + 1], b1[:, i:i + 1], ALU.mult, ALU.add),
                   deps=[r1w, b1w, x1b_war])
            xr_war[s] = xb
            tr_last = None
            for c in range(NCH):
                tr_last = T(lambda e, c=c: e.transpose(
                    pT[:, c * 128:(c + 1) * 128], x2b_t[:, 0, c * 128:(c + 1) * 128],
                    eye_sb[:]), deps=[xb, pT_war, smalls_y])
            x1b_war = tr_last
            ev = S(lambda e, i=i: e.copy(
                x1t[:, :, i * 128:(i + 1) * 128],
                pT[:].rearrange("p (c n) -> p c n", n=128)), deps=[tr_last])
            pT_war = ev
            x1t_rdy.append(ev)

        p2a_ev = {}

        def emit_P2a(i):
            nonlocal kraw_war, vraw_war, last_kmm, last_vmm
            slab_done = []
            for sl in range(4):
                last = None
                for c in range(NCH):
                    last = T(lambda e, i=i, sl=sl, c=c: e.matmul(
                        pbank[sl][:], x1t[:, c, i * 128:(i + 1) * 128],
                        wa[:, c, sl * 512:(sl + 1) * 512],
                        start=(c == 0), stop=False, skip_group_check=(c != 0)),
                        deps=[x1t_rdy[i], wkv_lds[c], pbank_war[sl]])
                last = T(lambda e, sl=sl: e.matmul(
                    pbank[sl][:], epad_sb[0:1, :],
                    kvb_sb[0:1, sl * 512:(sl + 1) * 512],
                    start=False, stop=True),
                    deps=[smalls_g, smalls_y])
                slab_done.append(last)
                if sl < 2:
                    last_kmm = last
                else:
                    last_vmm = last
            evk, evv = [], []
            for sl in range(2):
                evk.append(S(lambda e, sl=sl: e.copy(
                    kraw[:, sl * 512:(sl + 1) * 512], pbank[sl][:]),
                    deps=[slab_done[sl], kraw_war]))
                pbank_war[sl] = evk[sl]
            for sl in range(2, 4):
                evv.append(S(lambda e, sl=sl: e.copy(
                    vraw[:, (sl - 2) * 512:(sl - 1) * 512],
                    pbank[sl][:]), deps=[slab_done[sl], vraw_war]))
                pbank_war[sl] = evv[sl - 2]
            p2a_ev[i] = (evk, evv)

        def emit_P2b(i):
            nonlocal kraw_war, vraw_war, scr_war, pending_kv
            evk, evv = p2a_ev.pop(i)
            if pending_kv is not None:
                emit_kv(*pending_kv)
            slot = i % 2
            uk_ready = uv_ready = None
            for (raw, udst, evd) in ((kraw, uk, evk), (vraw, uv, evv)):
                ssq = S(lambda e, raw=raw: e.square(scr[:], raw[:]),
                        deps=evd + [scr_war])
                red_q = V(lambda e: e.tensor_reduce(
                    st_q[:], scr[:].rearrange("p (h d) -> p h d", d=HD), AX.X, ALU.add),
                    deps=[ssq])
                red_s = V(lambda e, raw=raw: e.tensor_reduce(
                    st_s[:], raw[:].rearrange("p (h d) -> p h d", d=HD), AX.X, ALU.add),
                    deps=evd)
                vm = V(lambda e: e.tensor_scalar(st_m[:], st_s[:], 1.0 / HD, None,
                                                 ALU.mult), deps=[red_s])
                vv1 = V(lambda e: e.tensor_scalar(st_v[:], st_q[:], 1.0 / HD, None,
                                                  ALU.mult), deps=[red_q])
                vv2 = V(lambda e: e.tensor_tensor(st_s[:], st_m[:], st_m[:],
                                                  ALU.mult), deps=[vm])
                vv3 = V(lambda e: e.tensor_tensor(st_v[:], st_v[:], st_s[:],
                                                  ALU.subtract), deps=[vv1, vv2])
                ssd = S(lambda e: e.activation(st_v[:], st_v[:], AF.Sqrt, bias=eps_c[:]),
                        deps=[vv3])
                vrr = V(lambda e: e.reciprocal(st_r[:], st_v[:]), deps=[ssd])
                uo = V(lambda e, udst=udst, slot=slot, raw=raw: e.tensor_tensor(
                    udst[:, slot, :].rearrange("p (h c) -> p h c", c=HP)[:, :, 0:HD],
                    raw[:].rearrange("p (h d) -> p h d", d=HD),
                    st_r[:].unsqueeze(2).broadcast_to([128, H, HD]), ALU.mult),
                    deps=[vrr, u_war[slot]] + ms)
                ao = V(lambda e, udst=udst, slot=slot: e.tensor_tensor(
                    udst[:, slot, :].rearrange("p (h c) -> p h c", c=HP)[:, :, HD + 1:HD + 2],
                    st_m[:].unsqueeze(2), st_r[:].unsqueeze(2), ALU.mult),
                    deps=[vm, vrr, u_war[slot]])
                scr_war = red_q
                if raw is kraw:
                    uk_ready = ao
                    uk_r2 = uo
                    kraw_war = uo
                else:
                    uv_ready = ao
                    uv_r2 = uo
                    vraw_war = uo
            pending_kv = (i, uk_ready, uv_ready, uk_r2, uv_r2)

        for j in range(NT + 3):
            if j < NT:
                emit_P1(j)
            if j >= 3:
                emit_P2b(j - 3)
            if 2 <= j < NT + 2:
                emit_P2a(j - 2)
        emit_kv(*pending_kv)

        ev1 = V(lambda e: e.tensor_copy(ccs[:, 0:7 * HP], pE[0:HP, 0:7 * HP]),
                deps=[kvmm_last])
        ev2 = V(lambda e: e.tensor_copy(ccs[:, 7 * HP:14 * HP], pF[0:HP, 0:7 * HP]),
                deps=[kvmm_last])
        ev3 = V(lambda e: e.tensor_copy(ccs[:, 14 * HP:16 * HP], pG[0:HP, 0:2 * HP]),
                deps=[kvmm_last])
        dcc = GD(lambda e: e.dma_start(out=cc_in[:], in_=ccs[:]),
                 deps=[ev1, ev2, ev3])
        P.wait("g", *dcc)
        ccv = P.cnt["sC"] = P.val("sC") + 1
        P.q["g"].append(lambda e: e.collective_compute(
            "AllReduce", ALU.add,
            replica_groups=[[0, 1], [2, 3], [4, 5], [6, 7]],
            ins=[cc_in[:]], outs=[cc_out[:]]).then_inc(SEMS["sC"], 1))
        cc_done = ("sC", ccv)

        p1_ld = GD(lambda e: e.dma_start(
            out=wa[:, :, 0:C],
            in_=p1T.rearrange("(c p) n -> p c n", p=128)), deps=[last_kmm])
        p2_ld = GD(lambda e: e.dma_start(
            out=wa[:, :, C:2 * C],
            in_=p2T.rearrange("(c p) n -> p c n", p=128)), deps=[last_vmm])
        dkv = GD(lambda e: e.dma_start(out=kvsum[:], in_=cc_out[:]), deps=[cc_done])
        dsv64 = YK(lambda e: e.dma_start(out=ccs[0:1, :], in_=kvsum[64:65, :]),
                   deps=[dkv, dcc])
        dsv65 = YK(lambda e: e.dma_start(out=ccs[32:33, :], in_=kvsum[65:66, :]),
                   deps=[dkv, dcc])

        q_evs = {}
        q_state = {"last_qmm": None, "bw": [pbank_war[0], pbank_war[1]]}

        def emit_q(b, extra=()):
            qs = b % NQ
            q_ev = []
            for oc in range(NCH):
                bnk = pA if oc % 2 == 0 else pB
                deps = [wq_ld, x1t_rdy[(b + 1) * TPB - 1],
                        q_state["bw"][oc % 2]] + list(extra)
                last = None
                for c in range(NCH):
                    last = T(lambda e, oc=oc, c=c, b=b, bnk=bnk: e.matmul(
                        bnk[:, 0:BLK], wt_sb[:, c, oc * 128:(oc + 1) * 128],
                        x1t[:, c, b * BLK:(b + 1) * BLK],
                        start=(c == 0), stop=(c == NCH - 1),
                        skip_group_check=not (c == 0 or c == NCH - 1)), deps=deps)
                q_state["last_qmm"] = last
                qe = S(lambda e, oc=oc, qs=qs, bnk=bnk: e.activation(
                    qbl[:, qs, oc, :], bnk[:, 0:BLK], AF.Identity,
                    bias=qb_sb[:, oc:oc + 1]), deps=[last, smalls_y])
                q_ev.append(qe)
                q_state["bw"][oc % 2] = qe
            q_evs[b] = q_ev

        for b in range(min(NB, NQ)):
            emit_q(b)
        wt_ld = None

        pair_rdy = [None] * NCH
        kvf_ready = None
        a5_prev = None
        rw_war = None
        for h in range(H):
            bnk = pE if h < 8 else pF
            c0 = h * HP
            ra1 = V(lambda e, c0=c0: e.tensor_scalar(
                rscr66[32:33, :], ccs[32:33, c0:c0 + HD],
                ccs[32:33, c0 + HD + 1:c0 + HD + 2], None, ALU.subtract),
                deps=[dsv65, smalls_y, rw_war])
            ra2 = V(lambda e: e.tensor_tensor(
                rscr66[32:33, :], rscr66[32:33, :], vng66_sb[32:33, :], ALU.mult),
                deps=[ra1])
            ra3 = V(lambda e, c0=c0: e.tensor_scalar(
                rscr66b[32:33, :], c4vb66[32:33, :],
                ccs[32:33, c0 + HD:c0 + HD + 1], None, ALU.mult),
                deps=[dsv65, v_c4vb, rw_war])
            ra4 = V(lambda e, h=h: e.tensor_tensor(
                rows66[32:33, h * HD:(h + 1) * HD], rscr66[32:33, :],
                rscr66b[32:33, :], ALU.add), deps=[ra2, ra3] + ms)
            rb1 = V(lambda e, c0=c0: e.tensor_scalar(
                rscr66[0:1, :], ccs[0:1, c0:c0 + HD],
                ccs[0:1, c0 + HD + 1:c0 + HD + 2], None, ALU.subtract),
                deps=[dsv64, rw_war])
            rb2 = V(lambda e: e.tensor_tensor(
                rscr66[0:1, :], rscr66[0:1, :], vng66_sb[0:1, :], ALU.mult),
                deps=[rb1])
            rb3 = V(lambda e, h=h: e.tensor_tensor(
                rows66[0:1, h * HD:(h + 1) * HD], rscr66[0:1, :],
                c4vb66[0:1, :], ALU.add),
                deps=[rb2, v_c4vb] + ms)
            rw_war = rb3
            rmm = T(lambda e, h=h, bnk=bnk: e.matmul(
                bnk[0:HD, (h % 8) * 64:(h % 8) * 64 + 64],
                kn2_sb[:], rows66[:, h * HD:(h + 1) * HD],
                start=True, stop=True),
                deps=[ra4, rb3, ev1, ev2, smalls_g, v_kn2n])
            a1 = V(lambda e, c0=c0: e.tensor_tensor(
                e1[:], kvsum[0:HD, c0:c0 + HD], G2_sb[:], ALU.mult),
                deps=[dkv, v_g2, kvf_ready])
            a2 = V(lambda e, c0=c0: e.tensor_scalar(
                e2[:], G2_sb[:], kvsum[0:HD, c0 + HD + 1:c0 + HD + 2], None,
                ALU.mult), deps=[dkv, v_g2])
            a3 = V(lambda e: e.tensor_tensor(e1[:], e1[:], e2[:], ALU.subtract),
                   deps=[a1, a2])
            a4 = V(lambda e, c0=c0: e.tensor_scalar(
                e2[:], T1_sb[:], kvsum[0:HD, c0 + HD:c0 + HD + 1], None,
                ALU.mult), deps=[v_t1, a3])
            a5 = V(lambda e: e.tensor_tensor(e1[:], e1[:], e2[:], ALU.add),
                   deps=[a4])
            a6 = V(lambda e, h=h, bnk=bnk: e.tensor_tensor(
                e1[:], e1[:], bnk[0:HD, (h % 8) * 64:(h % 8) * 64 + 64], ALU.add),
                deps=[a5, rmm])
            kvf_ready = V(lambda e, h=h: e.scalar_tensor_tensor(
                kvf_all[:, h * HD:(h + 1) * HD],
                e1[:], tempq[:, h:h + 1], ab_sb[:, h * HD:(h + 1) * HD],
                ALU.mult, ALU.add), deps=[a6, v_tempq] + ms)
            if h % 2 == 0:
                a5_prev = kvf_ready
            else:
                j = h // 2
                YK(lambda e, j=j: e.dma_start(
                    out=kvf2[0:HD, j * 128:j * 128 + HD],
                    in_=kvf_all[:, (2 * j) * HD:(2 * j + 1) * HD]),
                    deps=[a5_prev] + ms)
                pair_rdy[j] = YK(lambda e, j=j: e.dma_start(
                    out=kvf2[HD:128, j * 128 + HD:(j + 1) * 128],
                    in_=kvf_all[:, (2 * j + 1) * HD:(2 * j + 2) * HD]),
                    deps=[kvf_ready])

        attp_war = None
        hb_war = None
        xr2_war = xr_war
        zb_war = None
        yb_war = None
        ye_war = None
        xpb_war = [None, xr_war[1], None]
        xv_war = None
        pE_war = None
        x2b_war = [x1b_war, None, None]
        xob_war = None
        pT2_war = pT_war
        psum_war = {"pC": None, "pD": None, "pF": None, "pG": ev3}
        p2_banks = {0: ("pG", pG, "pC", pC), 1: ("pD", pD, "pF", pF)}
        pend = []
        p1_ev = []

        def emit_p2z_T(i, att_ev_l, p1_ev_l):
            nonlocal hb_war
            t2 = i % TPB
            par = i % 3
            n0, b0, n1, b1_ = p2_banks[i % 2]
            ldx2 = YX(lambda e, i=i: e.dma_start(
                out=xr[:, 0, :], in_=x_in[i * 128:(i + 1) * 128, :]),
                deps=[xr2_war[0]])
            zdone = []
            for sl, (nm, bnk) in ((0, (n0, b0)), (1, (n1, b1_))):
                deps = [p1_ev_l[-1], p2_ld, smalls_g, psum_war[nm]]
                if nm in ("pC", "pD"):
                    deps.append(att_ev_l[-1])
                last = None
                for c in range(NCH):
                    last = T(lambda e, sl=sl, c=c, t2=t2, bnk=bnk: e.matmul(
                        bnk[:], hb[:, c, t2 * 128:(t2 + 1) * 128],
                        wa[:, c, C + sl * 512:C + (sl + 1) * 512],
                        start=(c == 0), stop=False,
                        skip_group_check=(c != 0)), deps=deps)
                for cc in range(4):
                    c0 = sl * 4 + cc
                    last = T(lambda e, sl=sl, cc=cc, c0=c0, i=i, bnk=bnk: e.matmul(
                        bnk[:, cc * 128:(cc + 1) * 128],
                        x1t[:, c0, i * 128:(i + 1) * 128], eye_sb[:],
                        start=False, stop=False, skip_group_check=True))
                last = T(lambda e, sl=sl, bnk=bnk: e.matmul(
                    bnk[:], epad_sb[0:1, :],
                    p2b_sb[0:1, sl * 512:(sl + 1) * 512],
                    start=False, stop=True))
                zdone.append(last)
            hb_war = zdone[1]
            return (i, par, n0, b0, n1, b1_, ldx2, zdone)

        def emit_p2z_VS(st):
            nonlocal zb_war, last_sqrt
            i, par, n0, b0, n1, b1_, ldx2, zdone = st
            z0 = S(lambda e, b0=b0: e.copy(zb[:, 0:512], b0[:]),
                   deps=[zdone[0], zb_war])
            z1 = S(lambda e, b1_=b1_: e.copy(zb[:, 512:1024], b1_[:]),
                   deps=[zdone[1]])
            psum_war[n0] = z0
            psum_war[n1] = z1
            zb_war = z1
            c1 = V(lambda e: e.bn_stats(bnst[:, 0:6], zb[:, 0:512]),
                   deps=[z0, z1, last_sqrt])
            c2 = V(lambda e: e.bn_stats(bnst[:, 6:12], zb[:, 512:1024]), deps=[c1])
            c3 = V(lambda e: e.bn_aggr(mv2[:], bnst[:]), deps=[c1, c2])
            c4 = S(lambda e: e.activation(sd_c[:], mv2[:, 1:2], AF.Sqrt,
                                          bias=eps_c[:]), deps=[c3])
            last_sqrt = c4
            c5 = V(lambda e: e.reciprocal(rc_c[:], sd_c[:]), deps=[c4])
            c6 = V(lambda e: e.tensor_tensor(mrc_c[:], mv2[:, 0:1], rc_c[:],
                                             ALU.mult), deps=[c5])
            c7 = V(lambda e, i=i, par=par: e.tensor_scalar(
                xpb[:, par, :], xr[:, 0, :], mrc_c[:], None, ALU.subtract),
                deps=[c6, xpb_war[par], ldx2])
            xr2_war[0] = c7
            c8 = V(lambda e, par=par: e.scalar_tensor_tensor(
                xpb[:, par, :], zb[:], rc_c[:], xpb[:, par, :], ALU.mult, ALU.add),
                deps=[c7])
            d1_ = V(lambda e, par=par: e.bn_stats(bnst[:, 0:6], xpb[:, par, 0:512]),
                    deps=[c8, last_sqrt])
            d2_ = V(lambda e, par=par: e.bn_stats(bnst[:, 6:12],
                                                  xpb[:, par, 512:1024]), deps=[d1_])
            d3_ = V(lambda e: e.bn_aggr(mv2[:], bnst[:]), deps=[d1_, d2_])
            d4_ = S(lambda e: e.activation(sd_c[:], mv2[:, 1:2], AF.Sqrt,
                                           bias=eps_c[:]), deps=[d3_])
            last_sqrt = d4_
            d5_ = V(lambda e: e.reciprocal(rc_c[:], sd_c[:]), deps=[d4_])
            d6_ = V(lambda e: e.tensor_scalar(mrc_c[:], mv2[:, 0:1], rc_c[:],
                                              -1.0, ALU.mult, ALU.mult), deps=[d5_])
            d7_ = V(lambda e, par=par: e.tensor_scalar(
                x2b_t[:, par, :], xpb[:, par, :], rc_c[:], mrc_c[:],
                ALU.mult, ALU.add), deps=[d6_, x2b_war[par]])
            return (i, par, d7_)

        def emit_tail_head(pend):
            nonlocal pT2_war
            i, par, d7v = pend
            trl = None
            for c in range(NCH):
                trl = T(lambda e, c=c, par=par: e.transpose(
                    pT[:, c * 128:(c + 1) * 128],
                    x2b_t[:, par, c * 128:(c + 1) * 128], eye_sb[:]),
                    deps=[d7v, pT2_war])
            x2b_war[par] = trl
            evt = S(lambda e, i=i: e.copy(
                x1t[:, :, i * 128:(i + 1) * 128],
                pT[:].rearrange("p (c n) -> p c n", n=128)), deps=[trl])
            pT2_war = evt
            return evt

        def emit_tail_xv(pend, evt, p1_ev_l):
            nonlocal pE_war, xv_war
            i, par, d7v = pend
            xvl = None
            for c in range(NCH):
                xvl = T(lambda e, c=c, i=i: e.matmul(
                    pE[0:32, 0:128], vr_sb[:, c, :],
                    x1t[:, c, i * 128:(i + 1) * 128],
                    start=(c == 0), stop=(c == NCH - 1),
                    skip_group_check=not (c == 0 or c == NCH - 1)),
                    deps=[evt, smalls_g, p1_ev_l[-2], p1_ev_l[-1], pE_war,
                          kvf_ready])
            xvev = S(lambda e: e.copy(xv_sb[0:32, :], pE[0:32, 0:128]),
                     deps=[xvl, xv_war] + ms)
            pE_war = xvev
            return xvev

        def emit_tail_rest(pend, evt, xvev, p1_ev_l):
            nonlocal xv_war, ye_war, yb_war, xob_war, last_sqrt
            i, par, d7v = pend
            ydone = []
            for sl, bnk in ((0, pA), (1, pB)):
                deps = [evt, wt_ld, ye_war, q_evs[max(q_evs)][-1]]
                last = None
                for c in range(NCH):
                    last = T(lambda e, sl=sl, c=c, i=i, bnk=bnk: e.matmul(
                        bnk[:], x1t[:, c, i * 128:(i + 1) * 128],
                        wt_sb[:, c, sl * 512:(sl + 1) * 512],
                        start=(c == 0), stop=False,
                        skip_group_check=(c != 0)), deps=deps)
                last = T(lambda e, sl=sl, bnk=bnk: e.matmul(
                    bnk[:], xv_sb[:], ct_sb[:, sl * 512:(sl + 1) * 512],
                    start=False, stop=False, skip_group_check=True),
                    deps=[xvev])
                last = T(lambda e, sl=sl, bnk=bnk: e.matmul(
                    bnk[:], epad_sb[0:1, :],
                    fb_sb[0:1, sl * 512:(sl + 1) * 512],
                    start=False, stop=True))
                ydone.append(last)
            xv_war = ydone[1]
            ye0 = S(lambda e: e.copy(yb[:, 0:512], pA[:]), deps=[ydone[0], yb_war])
            ye1 = S(lambda e: e.copy(yb[:, 512:1024], pB[:]), deps=[ydone[1]])
            ye_war = ye1
            q_state["bw"][0] = ye0
            q_state["bw"][1] = ye1
            f1 = V(lambda e: e.bn_stats(bnst[:, 0:6], yb[:, 0:512]),
                   deps=[ye0, last_sqrt])
            f2 = V(lambda e: e.bn_stats(bnst[:, 6:12], yb[:, 512:1024]),
                   deps=[ye1, f1])
            f3 = V(lambda e: e.bn_aggr(mv2[:], bnst[:]), deps=[f1, f2])
            f4 = S(lambda e: e.activation(sd_c[:], mv2[:, 1:2], AF.Sqrt,
                                          bias=eps_c[:]), deps=[f3])
            last_sqrt = f4
            f5 = V(lambda e: e.reciprocal(rc_c[:], sd_c[:]), deps=[f4])
            f6 = V(lambda e: e.tensor_tensor(mrc_c[:], mv2[:, 0:1], rc_c[:],
                                             ALU.mult), deps=[f5])
            f7 = V(lambda e, par=par: e.tensor_scalar(
                xob[:], xpb[:, par, :], mrc_c[:], None, ALU.subtract),
                deps=[f6, xob_war])
            xpb_war[par] = f7
            f8 = V(lambda e: e.scalar_tensor_tensor(
                xob[:], yb[:], rc_c[:], xob[:], ALU.mult, ALU.add), deps=[f7])
            yb_war = f8
            st = YO(lambda e, i=i: e.dma_start(
                out=out_d[i * 128:(i + 1) * 128, :], in_=xob[:]), deps=[f8])
            xob_war = st

        for b in range(NB):
            qs = b % NQ
            att_ev = []
            am = None
            for j in range(NCH):
                bnk = pC if j % 2 == 0 else pD
                deps = [pair_rdy[j], q_evs[b][-1], smalls_y,
                        psum_war["pC" if j % 2 == 0 else "pD"],
                        pbank_war[2 + (j % 2)]]
                if len(att_ev) >= 2:
                    deps.append(att_ev[-2])
                T(lambda e, j=j, qs=qs, bnk=bnk: e.matmul(
                    bnk[:, 0:BLK], kvf2[:, j * 128:(j + 1) * 128],
                    qbl[:, qs, j, :], start=True, stop=False), deps=deps)
                am = T(lambda e, j=j, b=b, bnk=bnk: e.matmul(
                    bnk[:, 0:BLK], eye_sb[:], x1t[:, j, b * BLK:(b + 1) * BLK],
                    start=False, stop=True), deps=[])
                att_ev.append(S(lambda e, j=j, bnk=bnk: e.copy(
                    attp[:, j, :], bnk[:, 0:BLK]), deps=[am, attp_war]))
            if b + NQ < NB:
                emit_q(b + NQ, extra=[am])
            if wt_ld is None and (b + NQ >= NB - 1):
                wt_ld = GD(lambda e: e.dma_start(
                    out=wt_sb[:], in_=wT.rearrange("(c p) n -> p c n", p=128)),
                    deps=[q_state["last_qmm"]])
            p1_ev = []
            last_p1mm = None
            for oc in range(NCH):
                bnk = pE if oc % 2 == 0 else pF
                deps = [p1_ld, att_ev[-1], pE_war,
                        psum_war["pF"] if oc % 2 == 1 else None]
                if len(p1_ev) >= 2:
                    deps.append(p1_ev[-2])
                last = None
                for c in range(NCH):
                    last = T(lambda e, oc=oc, c=c, bnk=bnk: e.matmul(
                        bnk[:, 0:BLK], wa[:, c, oc * 128:(oc + 1) * 128],
                        attp[:, c, :], start=(c == 0), stop=(c == NCH - 1),
                        skip_group_check=not (c == 0 or c == NCH - 1)), deps=deps)
                last_p1mm = last
                p1_ev.append(S(lambda e, oc=oc, bnk=bnk: e.activation(
                    hb[:, oc, :], bnk[:, 0:BLK], AF.Gelu,
                    bias=p1b_sb[:, oc:oc + 1]), deps=[last, hb_war, smalls_y]))
            attp_war = last_p1mm

            for t2 in range(TPB):
                i = b * TPB + t2
                if len(pend) >= 2:
                    old = pend.pop(0)
                    evt = emit_tail_head(old)
                else:
                    old = None
                st = emit_p2z_T(i, att_ev, p1_ev)
                if old is not None:
                    xvev = emit_tail_xv(old, evt, p1_ev)
                pend.append(emit_p2z_VS(st))
                if old is not None:
                    emit_tail_rest(old, evt, xvev, p1_ev)
        while pend:
            old = pend.pop(0)
            evt = emit_tail_head(old)
            xvev = emit_tail_xv(old, evt, p1_ev)
            emit_tail_rest(old, evt, xvev, p1_ev)

        for s in ("sY", "sYx", "sYo", "sYk"):
            P.wait("y", s, P.val(s))
        P.wait("g", "sGD", P.val("sGD"))

        with nc.Block() as block:
            @block.tensor
            def _(e):
                for f in P.q["t"]:
                    f(e)

            @block.vector
            def _(e):
                for f in P.q["v"]:
                    f(e)

            @block.scalar
            def _(e):
                for f in P.q["s"]:
                    f(e)

            @block.gpsimd
            def _(e):
                for f in P.q["g"]:
                    f(e)

            @block.sync
            def _(e):
                for f in P.q["y"]:
                    f(e)

    return nc


def make_inputs(core, x, qkv_w, qkv_b, kn_g, kn_b, vn_g, vn_b, temperature,
                attn_bias, p1_w, p1_b, p2_w, p2_b, W, b, V_r, Cm, TL):
    f32 = np.float32
    batch = core // 2
    half = core % 2
    xs = np.ascontiguousarray(x[batch, half * TL:(half + 1) * TL, :], dtype=f32)

    def rowpad(vec, width):
        a = np.zeros((128, width), dtype=f32)
        a[0, :] = vec
        return a

    def _rows66(vec):
        a = np.zeros((33, 64), dtype=f32)
        a[0, :] = vec
        a[32, :] = vec
        return a

    def _kn2(row64, row65):
        a = np.zeros((128, 64), dtype=f32)
        a[0, :] = row64
        a[32, :] = row65
        return a

    return {
        "x_in": xs,
        "wkvT": np.ascontiguousarray(qkv_w[1024:3072].T, dtype=f32),
        "wqT": np.ascontiguousarray(qkv_w[0:1024].T, dtype=f32),
        "p1T": np.ascontiguousarray(p1_w.T, dtype=f32),
        "p2T": np.ascontiguousarray(p2_w.T, dtype=f32),
        "wT": np.ascontiguousarray(W.T, dtype=f32),
        "vrW": np.ascontiguousarray(V_r, dtype=f32),
        "ctp": np.concatenate([np.ascontiguousarray(Cm.T, dtype=f32),
                               np.zeros((96, 1024), dtype=f32)], axis=0),
        "qb_pp": np.ascontiguousarray(qkv_b[0:1024].reshape(8, 128).T, dtype=f32),
        "p1b_pp": np.ascontiguousarray(p1_b.reshape(8, 128).T, dtype=f32),
        "kvb_pad": np.ascontiguousarray(qkv_b[1024:3072].reshape(1, 2048), dtype=f32),
        "p2b_pad": np.ascontiguousarray(p2_b.reshape(1, 1024), dtype=f32),
        "fb_pad": np.ascontiguousarray(b.reshape(1, 1024), dtype=f32),

        "kng_pad": rowpad(kn_g, 64),
        "vng_pad": rowpad(vn_g, 64),
        "vnb_pad": rowpad(vn_b, 64),
        "vng66": _rows66(vn_g),
        "vnb66": _rows66(vn_b),
        "kn2_66": _kn2(kn_b, kn_g),
        "tempb": np.ascontiguousarray(
            np.broadcast_to(np.asarray(temperature, dtype=f32).reshape(1, H),
                            (HD, H))),
        "abt": np.ascontiguousarray(
            np.asarray(attn_bias, dtype=f32).transpose(1, 0, 2).reshape(HD, H * HD)),
        "eye_d": np.eye(128, dtype=BFNP),
        "epad_d": rowpad(np.ones(128, dtype=f32), 128).astype(BFNP),
    }


def kernel(L0_attn_x, L0_attn_qkv_w, L0_attn_qkv_b, L0_attn_kn_g, L0_attn_kn_b,
           L0_attn_vn_g, L0_attn_vn_b, L0_attn_temperature, L0_attn_attn_bias,
           L0_attn_p1_w, L0_attn_p1_b, L0_attn_p2_w, L0_attn_p2_b,
           L0_ffn_W, L0_ffn_b, L0_ffn_V_r, L0_ffn_C):
    NT = 16
    TL = NT * 128
    x = np.asarray(L0_attn_x, dtype=np.float32)
    args = [np.asarray(a, dtype=np.float32) for a in
            (L0_attn_qkv_w, L0_attn_qkv_b, L0_attn_kn_g, L0_attn_kn_b,
             L0_attn_vn_g, L0_attn_vn_b, L0_attn_temperature, L0_attn_attn_bias,
             L0_attn_p1_w, L0_attn_p1_b, L0_attn_p2_w, L0_attn_p2_b,
             L0_ffn_W, L0_ffn_b, L0_ffn_V_r, L0_ffn_C)]
    nc = build_nc(NT, QTOT=2 * TL)
    in_maps = [make_inputs(c, x, *args, TL=TL) for c in range(NCORE)]
    res = run_bass_kernel_spmd(nc, in_maps, core_ids=list(range(NCORE)))
    out = np.empty((4, 4096, 1024), dtype=np.float32)
    for c in range(NCORE):
        batch, half = c // 2, c % 2
        out[batch, half * TL:(half + 1) * TL, :] = np.asarray(res.results[c]["out"])
    return out


# version 20
# speedup vs baseline: 1.0093x; 1.0093x over previous
import numpy as np
import ml_dtypes
import concourse.bass as bass
import concourse.mybir as mybir
from concourse.bass_utils import run_bass_kernel_spmd

F32 = mybir.dt.float32
BF16 = mybir.dt.bfloat16
AF = mybir.ActivationFunctionType
ALU = mybir.AluOpType
AX = mybir.AxisListType
BFNP = ml_dtypes.bfloat16

C = 1024
NCH = 8
H = 16
HD = 64
HP = HD + 2
EPS = 1e-5
NCORE = 8

ENG_SEM = {"t": "sT", "v": "sV", "s": "sS", "g": "sG", "y": "sY"}

SEMS = {}


class PB:

    def __init__(self):
        self.q = {e: [] for e in ENG_SEM}
        self.cnt = {}
        self.waited = {e: {} for e in ENG_SEM}
        self.last_dma = {}

    def val(self, sem):
        return self.cnt.get(sem, 0)

    def wait(self, eng, sem, v):
        if v is None or v <= 0:
            return
        assert v <= self.val(sem), f"wait on future: {eng} {sem} {v} > {self.val(sem)}"
        if self.waited[eng].get(sem, 0) >= v:
            return
        self.waited[eng][sem] = v
        self.q[eng].append(lambda e, _s=sem, _v=v: e.wait_ge(SEMS[_s], _v))

    def op(self, eng, emit, deps=(), dma=None):
        deps = [d for d in deps if d is not None]
        if dma:
            sem = dma if isinstance(dma, str) else (
                "sGD" if eng == "g" else "sY")
            prev = self.last_dma.get(sem)
            if prev is not None:
                deps.append(prev)
            by = 16
        else:
            sem = ENG_SEM[eng]
            by = 1
        for (s, v) in deps:
            self.wait(eng, s, v)
        v = self.cnt[sem] = self.val(sem) + by
        if dma:
            self.last_dma[sem] = (sem, v)
        self.q[eng].append(lambda e, _f=emit, _s=sem, _by=by: _f(e).then_inc(SEMS[_s], _by))
        return (sem, v)


def build_nc(NT, QTOT):
    global SEMS
    TL = NT * 128
    NB = max(1, TL // 512)
    BLK = TL // NB
    TPB = NT // NB

    nc = bass.Bass("TRN2", target_bir_lowering=False, debug=False,
                   dynamic_dma_scratch_size=8192)

    D = lambda n, sh, dt=F32, kind="ExternalInput": nc.dram_tensor(n, sh, dt, kind=kind).ap()
    x_in = D("x_in", [TL, C])
    wkvT = D("wkvT", [C, 2 * C])
    wqT = D("wqT", [C, C])
    p1T = D("p1T", [C, C])
    p2T = D("p2T", [C, C])
    wT = D("wT", [C, C])
    vrW = D("vrW", [C, 32])
    ctp = D("ctp", [128, C])
    qb_pp = D("qb_pp", [128, NCH])
    p1b_pp = D("p1b_pp", [128, NCH])
    kvb_pad = D("kvb_pad", [1, 2 * C])
    p2b_pad = D("p2b_pad", [1, C])
    fb_pad = D("fb_pad", [1, C])
    kng_pad = D("kng_pad", [128, HD])
    vng_pad = D("vng_pad", [128, HD])
    vnb_pad = D("vnb_pad", [128, HD])
    vng66 = D("vng66", [33, HD])
    vnb66 = D("vnb66", [33, HD])
    kn2_66 = D("kn2_66", [128, HD])
    tempb = D("tempb", [HD, H])
    abt = D("abt", [HD, H * HD])
    eye_d = D("eye_d", [128, 128], BF16)
    epad_d = D("epad_d", [128, 128], BF16)
    out_d = D("out", [TL, C], F32, kind="ExternalOutput")
    cc_in = D("cc_in", [HP, H * HP], F32, kind="Internal")
    cc_out = D("cc_out", [HP, H * HP], F32, kind="Internal")

    import contextlib
    es = contextlib.ExitStack()
    with es:
        SB = lambda n, sh, dt=F32: es.enter_context(nc.sbuf_tensor(n, sh, dt))
        PS = lambda n, sh, dt=F32: es.enter_context(nc.psum_tensor(n, sh, dt))

        x1t = SB("x1t", [128, NCH, TL], BF16)
        wa = SB("wa", [128, NCH, 2048], BF16)
        wt_sb = SB("wt_sb", [128, NCH, C], BF16)
        vr_sb = SB("vr_sb", [128, NCH, 32], BF16)
        ct_sb = SB("ct_sb", [128, C], BF16)
        qb_sb = SB("qb_sb", [128, NCH])
        p1b_sb = SB("p1b_sb", [128, NCH])
        kvb_sb = SB("kvb_sb", [1, 2 * C], BF16)
        p2b_sb = SB("p2b_sb", [1, C], BF16)
        fb_sb = SB("fb_sb", [1, C], BF16)
        kngp_sb = SB("kngp_sb", [128, HD])
        vngp_sb = SB("vngp_sb", [128, HD])
        vnbp_sb = SB("vnbp_sb", [128, HD])
        vng66_sb = SB("vng66_sb", [33, HD])
        c4vb66 = SB("c4vb66", [33, HD])
        kn2_sb = SB("kn2_sb", [128, HD], BF16)
        rows66 = SB("rows66", [128, H * HD], BF16)
        rscr66 = SB("rscr66", [33, HD])
        rscr66b = SB("rscr66b", [33, HD])
        tempq = SB("tempq", [HD, H])
        ab_sb = SB("ab_sb", [HD, H * HD])
        eye_sb = SB("eye_sb", [128, 128], BF16)
        epad_sb = SB("epad_sb", [128, 128], BF16)
        G2_sb = SB("G2_sb", [HD, HD])
        T1_sb = SB("T1_sb", [HD, HD])
        kvf_all = SB("kvf_all", [HD, H * HD], BF16)
        kvf2 = SB("kvf2", [128, NCH * 128], BF16)
        ccs = SB("ccs", [HP, H * HP])
        kvsum = SB("kvsum", [HP, H * HP])
        r1 = SB("r1", [128, NT])
        b1 = SB("b1", [128, NT])
        xr = SB("xr", [128, 1, C])
        x2b_t = SB("x2b_t", [128, 3, C], BF16)
        kraw = SB("kraw", [128, C])
        vraw = SB("vraw", [128, C])
        scr = SB("scr", [128, C])
        uk = SB("uk", [128, 2, H * HP], BF16)
        uv = SB("uv", [128, 2, H * HP], BF16)
        st_s = SB("st_s", [128, H])
        st_q = SB("st_q", [128, H])
        st_m = SB("st_m", [128, H])
        st_v = SB("st_v", [128, H])
        st_r = SB("st_r", [128, H])
        mv2 = SB("mv2", [128, 2])
        bnst = SB("bnst", [128, 12])
        sd_c = SB("sd_c", [128, 1])
        rc_c = SB("rc_c", [128, 1])
        mrc_c = SB("mrc_c", [128, 1])
        eps_c = SB("eps_c", [128, 1])
        qbl = SB("qbl", [128, NB, NCH, BLK], BF16)
        attp = SB("attp", [128, NCH, BLK], BF16)
        hb = SB("hb", [128, NCH, BLK], BF16)
        xv_sb = SB("xv_sb", [128, 128], BF16)
        zb = SB("zb", [128, C])
        yb = SB("yb", [128, C])
        xpb = SB("xpb", [128, 3, C])
        e1 = SB("e1", [HD, HD])
        e2 = SB("e2", [HD, HD])
        xob = SB("xob", [128, C])
        NQ = NB

        pA = PS("pA", [128, 512]); pB = PS("pB", [128, 512])
        pC = PS("pC", [128, 512]); pD = PS("pD", [128, 512])
        pE = PS("pE", [128, 512]); pF = PS("pF", [128, 512])
        pG = PS("pG", [128, 512])
        pT = PS("pT", [128, 1024], BF16)

        SEMS = {n: es.enter_context(nc.semaphore(n)) for n in
                ("sT", "sV", "sS", "sG", "sGD", "sY", "sYx", "sYo", "sYk", "sC")}

        P = PB()
        T = lambda emit, deps=(): P.op("t", emit, deps)
        V = lambda emit, deps=(): P.op("v", emit, deps)
        S = lambda emit, deps=(): P.op("s", emit, deps)
        G = lambda emit, deps=(): P.op("g", emit, deps)
        YD = lambda emit, deps=(): P.op("y", emit, deps, dma="sY")
        YX = lambda emit, deps=(): P.op("y", emit, deps, dma="sYx")
        YO = lambda emit, deps=(): P.op("y", emit, deps, dma="sYo")
        YK = lambda emit, deps=(): P.op("y", emit, deps, dma="sYk")
        GD = lambda emit, deps=(): P.op("g", emit, deps, dma="sGD")

        for dst, src in ((qb_sb, qb_pp), (p1b_sb, p1b_pp), (kngp_sb, kng_pad),
                         (vngp_sb, vng_pad), (vnbp_sb, vnb_pad), (vng66_sb, vng66),
                         (c4vb66, vnb66), (tempq, tempb), (ab_sb, abt),
                         (eye_sb, eye_d), (epad_sb, epad_d)):
            smalls_y = YD(lambda e, d=dst, s=src: e.dma_start(out=d[:], in_=s[:]))
        for dst, src in ((kvb_sb, kvb_pad), (p2b_sb, p2b_pad), (fb_sb, fb_pad),
                         (kn2_sb, kn2_66), (ct_sb, ctp)):
            smalls_g = GD(lambda e, d=dst, s=src: e.dma_start(out=d[:], in_=s[:]))
        smalls_g = GD(lambda e: e.dma_start(
            out=vr_sb[:], in_=vrW.rearrange("(c p) r -> p c r", p=128)))
        wkv_lds = []
        for c in range(NCH):
            wkv_lds.append(GD(lambda e, c=c: e.dma_start(
                out=wa[:, c, :], in_=wkvT[c * 128:(c + 1) * 128, :])))
        wq_ld = GD(lambda e: e.dma_start(
            out=wt_sb[:], in_=wqT.rearrange("(c p) n -> p c n", p=128)))

        ms = []
        for fn in (lambda e: e.memset(uk[:], 0.0),
                   lambda e: e.memset(uv[:], 0.0),
                   lambda e: e.memset(
                       uk[:].rearrange("p b (h c) -> p b h c", c=HP)
                       [:, :, :, HD:HD + 1], 1.0),
                   lambda e: e.memset(
                       uv[:].rearrange("p b (h c) -> p b h c", c=HP)
                       [:, :, :, HD:HD + 1], 1.0),
                   lambda e: e.memset(kvf2[:], 0.0),
                   lambda e: e.memset(rows66[:], 0.0),
                   lambda e: e.memset(xv_sb[:], 0.0),
                   lambda e: e.memset(eps_c[:], EPS)):
            ms.append(V(fn, deps=ms[-1:]))
        v_tempq = V(lambda e: e.tensor_scalar(tempq[:], tempq[:], 1.0 / QTOT, None,
                                              ALU.mult), deps=[smalls_y])
        v_c4vb = V(lambda e: e.tensor_scalar(c4vb66[0:1, :], c4vb66[0:1, :],
                                             float(QTOT), None, ALU.mult),
                   deps=[smalls_y])
        v_kn2n = V(lambda e: e.tensor_scalar(kn2_sb[32:33, :], kn2_sb[32:33, :],
                                             -1.0, None, ALU.mult), deps=[smalls_g])
        t_g2 = T(lambda e: e.matmul(pG[0:HD, 0:HD], kngp_sb[:], vngp_sb[:],
                                    start=True, stop=True), deps=[smalls_y])
        t_t1 = T(lambda e: e.matmul(pG[0:HD, 64:128], kngp_sb[:], vnbp_sb[:],
                                    start=True, stop=True), deps=[smalls_y])
        v_g2 = V(lambda e: e.tensor_copy(G2_sb[:], pG[0:HD, 0:HD]), deps=[t_g2, t_t1])
        v_t1 = V(lambda e: e.tensor_copy(T1_sb[:], pG[0:HD, 64:128]), deps=[t_g2, t_t1])

        pbank = [pA, pB, pC, pD]
        pbank_war = [None] * 4
        u_war = [None, None]
        kraw_war = None
        vraw_war = None
        scr_war = None
        kv_first = {0: True, 1: True, 2: True}
        last_kmm = None
        last_vmm = None
        kvmm_last = None
        kv_banks = {0: pE, 1: pF, 2: pG}
        pending_kv = None

        def emit_kv(i, uk_ready, uv_ready, uk_r2, uv_r2):
            nonlocal kvmm_last
            slot = i % 2
            for h in range(H):
                bk = 0 if h < 7 else (1 if h < 14 else 2)
                col = (h - (0, 7, 14)[bk]) * HP
                first = kv_first[bk] and i == 0
                dep_extra = [v_g2, v_t1] if (first and bk == 2) else []
                if first:
                    kv_first[bk] = False
                lastmm = (i == NT - 1) and (h in (6, 13, 15))
                kvmm_last = T(lambda e, h=h, bk=bk, col=col, slot=slot, first=first,
                              lastmm=lastmm: e.matmul(
                    kv_banks[bk][0:HP, col:col + HP],
                    uk[:, slot, h * HP:(h + 1) * HP],
                    uv[:, slot, h * HP:(h + 1) * HP],
                    start=first, stop=lastmm,
                    skip_group_check=not (first or lastmm)),
                    deps=[uk_ready, uv_ready, uk_r2, uv_r2] + dep_extra)
            u_war[slot] = kvmm_last

        x1b_war = None
        pT_war = None
        xr_war = [None, None]
        x1t_rdy = []
        r1b1_v = None
        last_sqrt = None

        def emit_P1(i):
            nonlocal x1b_war, pT_war, r1b1_v, last_sqrt
            s = i % 2
            xt, xs = (xr, 0) if s == 0 else (xpb, 1)
            ldx = YX(lambda e, i=i, xt=xt, xs=xs: e.dma_start(
                out=xt[:, xs, :], in_=x_in[i * 128:(i + 1) * 128, :]),
                deps=[xr_war[s]])
            b1a = V(lambda e, xt=xt, xs=xs: e.bn_stats(
                bnst[:, 0:6], xt[:, xs, 0:512]), deps=[ldx, last_sqrt])
            b1b = V(lambda e, xt=xt, xs=xs: e.bn_stats(
                bnst[:, 6:12], xt[:, xs, 512:1024]), deps=[b1a])
            ag = V(lambda e: e.bn_aggr(mv2[:], bnst[:]), deps=[b1a, b1b])
            sq = S(lambda e: e.activation(sd_c[:], mv2[:, 1:2], AF.Sqrt, bias=eps_c[:]),
                   deps=[ag])
            last_sqrt = sq
            rc = V(lambda e: e.reciprocal(rc_c[:], sd_c[:]), deps=[sq])
            r1w = V(lambda e, i=i: e.tensor_copy(r1[:, i:i + 1], rc_c[:]), deps=[rc])
            b1w = V(lambda e, i=i: e.tensor_scalar(b1[:, i:i + 1], mv2[:, 0:1],
                                                   rc_c[:], -1.0, ALU.mult, ALU.mult),
                    deps=[rc])
            r1b1_v = b1w
            xb = V(lambda e, i=i, xt=xt, xs=xs: e.tensor_scalar(
                x2b_t[:, 0, :], xt[:, xs, :],
                r1[:, i:i + 1], b1[:, i:i + 1], ALU.mult, ALU.add),
                   deps=[r1w, b1w, x1b_war])
            xr_war[s] = xb
            tr_last = None
            for c in range(NCH):
                tr_last = T(lambda e, c=c: e.transpose(
                    pT[:, c * 128:(c + 1) * 128], x2b_t[:, 0, c * 128:(c + 1) * 128],
                    eye_sb[:]), deps=[xb, pT_war, smalls_y])
            x1b_war = tr_last
            ev = S(lambda e, i=i: e.copy(
                x1t[:, :, i * 128:(i + 1) * 128],
                pT[:].rearrange("p (c n) -> p c n", n=128)), deps=[tr_last])
            pT_war = ev
            x1t_rdy.append(ev)

        p2a_ev = {}

        def emit_P2a(i):
            nonlocal kraw_war, vraw_war, last_kmm, last_vmm
            slab_done = []
            for sl in range(4):
                last = None
                for c in range(NCH):
                    last = T(lambda e, i=i, sl=sl, c=c: e.matmul(
                        pbank[sl][:], x1t[:, c, i * 128:(i + 1) * 128],
                        wa[:, c, sl * 512:(sl + 1) * 512],
                        start=(c == 0), stop=False, skip_group_check=(c != 0)),
                        deps=[x1t_rdy[i], wkv_lds[c], pbank_war[sl]])
                last = T(lambda e, sl=sl: e.matmul(
                    pbank[sl][:], epad_sb[0:1, :],
                    kvb_sb[0:1, sl * 512:(sl + 1) * 512],
                    start=False, stop=True),
                    deps=[smalls_g, smalls_y])
                slab_done.append(last)
                if sl < 2:
                    last_kmm = last
                else:
                    last_vmm = last
            evk, evv = [], []
            for sl in range(2):
                evk.append(S(lambda e, sl=sl: e.copy(
                    kraw[:, sl * 512:(sl + 1) * 512], pbank[sl][:]),
                    deps=[slab_done[sl], kraw_war]))
                pbank_war[sl] = evk[sl]
            for sl in range(2, 4):
                evv.append(S(lambda e, sl=sl: e.copy(
                    vraw[:, (sl - 2) * 512:(sl - 1) * 512],
                    pbank[sl][:]), deps=[slab_done[sl], vraw_war]))
                pbank_war[sl] = evv[sl - 2]
            p2a_ev[i] = (evk, evv)

        def emit_P2b(i):
            nonlocal kraw_war, vraw_war, scr_war, pending_kv
            evk, evv = p2a_ev.pop(i)
            if pending_kv is not None:
                emit_kv(*pending_kv)
            slot = i % 2
            uk_ready = uv_ready = None
            for (raw, udst, evd) in ((kraw, uk, evk), (vraw, uv, evv)):
                ssq = S(lambda e, raw=raw: e.square(scr[:], raw[:]),
                        deps=evd + [scr_war])
                red_q = V(lambda e: e.tensor_reduce(
                    st_q[:], scr[:].rearrange("p (h d) -> p h d", d=HD), AX.X, ALU.add),
                    deps=[ssq])
                red_s = V(lambda e, raw=raw: e.tensor_reduce(
                    st_s[:], raw[:].rearrange("p (h d) -> p h d", d=HD), AX.X, ALU.add),
                    deps=evd)
                vm = V(lambda e: e.tensor_scalar(st_m[:], st_s[:], 1.0 / HD, None,
                                                 ALU.mult), deps=[red_s])
                vv1 = V(lambda e: e.tensor_scalar(st_v[:], st_q[:], 1.0 / HD, None,
                                                  ALU.mult), deps=[red_q])
                vv2 = V(lambda e: e.tensor_tensor(st_s[:], st_m[:], st_m[:],
                                                  ALU.mult), deps=[vm])
                vv3 = V(lambda e: e.tensor_tensor(st_v[:], st_v[:], st_s[:],
                                                  ALU.subtract), deps=[vv1, vv2])
                ssd = S(lambda e: e.activation(st_v[:], st_v[:], AF.Sqrt, bias=eps_c[:]),
                        deps=[vv3])
                vrr = V(lambda e: e.reciprocal(st_r[:], st_v[:]), deps=[ssd])
                uo = V(lambda e, udst=udst, slot=slot, raw=raw: e.tensor_tensor(
                    udst[:, slot, :].rearrange("p (h c) -> p h c", c=HP)[:, :, 0:HD],
                    raw[:].rearrange("p (h d) -> p h d", d=HD),
                    st_r[:].unsqueeze(2).broadcast_to([128, H, HD]), ALU.mult),
                    deps=[vrr, u_war[slot]] + ms)
                ao = V(lambda e, udst=udst, slot=slot: e.tensor_tensor(
                    udst[:, slot, :].rearrange("p (h c) -> p h c", c=HP)[:, :, HD + 1:HD + 2],
                    st_m[:].unsqueeze(2), st_r[:].unsqueeze(2), ALU.mult),
                    deps=[vm, vrr, u_war[slot]])
                scr_war = red_q
                if raw is kraw:
                    uk_ready = ao
                    uk_r2 = uo
                    kraw_war = uo
                else:
                    uv_ready = ao
                    uv_r2 = uo
                    vraw_war = uo
            pending_kv = (i, uk_ready, uv_ready, uk_r2, uv_r2)

        for j in range(NT + 3):
            if j < NT:
                emit_P1(j)
            if j >= 3:
                emit_P2b(j - 3)
            if 2 <= j < NT + 2:
                emit_P2a(j - 2)
        emit_kv(*pending_kv)

        ev1 = V(lambda e: e.tensor_copy(ccs[:, 0:7 * HP], pE[0:HP, 0:7 * HP]),
                deps=[kvmm_last])
        ev2 = V(lambda e: e.tensor_copy(ccs[:, 7 * HP:14 * HP], pF[0:HP, 0:7 * HP]),
                deps=[kvmm_last])
        ev3 = V(lambda e: e.tensor_copy(ccs[:, 14 * HP:16 * HP], pG[0:HP, 0:2 * HP]),
                deps=[kvmm_last])
        dcc = GD(lambda e: e.dma_start(out=cc_in[:], in_=ccs[:]),
                 deps=[ev1, ev2, ev3])
        P.wait("g", *dcc)
        ccv = P.cnt["sC"] = P.val("sC") + 1
        P.q["g"].append(lambda e: e.collective_compute(
            "AllReduce", ALU.add,
            replica_groups=[[0, 1], [2, 3], [4, 5], [6, 7]],
            ins=[cc_in[:]], outs=[cc_out[:]]).then_inc(SEMS["sC"], 1))
        cc_done = ("sC", ccv)

        p1_ld = GD(lambda e: e.dma_start(
            out=wa[:, :, 0:C],
            in_=p1T.rearrange("(c p) n -> p c n", p=128)), deps=[last_kmm])
        p2_ld = GD(lambda e: e.dma_start(
            out=wa[:, :, C:2 * C],
            in_=p2T.rearrange("(c p) n -> p c n", p=128)), deps=[last_vmm])
        dkv = GD(lambda e: e.dma_start(out=kvsum[:], in_=cc_out[:]), deps=[cc_done])
        dsv64 = YK(lambda e: e.dma_start(out=ccs[0:1, :], in_=kvsum[64:65, :]),
                   deps=[dkv, dcc])
        dsv65 = YK(lambda e: e.dma_start(out=ccs[32:33, :], in_=kvsum[65:66, :]),
                   deps=[dkv, dcc])

        q_evs = {}
        q_state = {"last_qmm": None, "bw": [pbank_war[0], pbank_war[1]]}

        def emit_q(b, extra=()):
            qs = b % NQ
            q_ev = []
            for oc in range(NCH):
                bnk = pA if oc % 2 == 0 else pB
                deps = [wq_ld, x1t_rdy[(b + 1) * TPB - 1],
                        q_state["bw"][oc % 2]] + list(extra)
                last = None
                for c in range(NCH):
                    last = T(lambda e, oc=oc, c=c, b=b, bnk=bnk: e.matmul(
                        bnk[:, 0:BLK], wt_sb[:, c, oc * 128:(oc + 1) * 128],
                        x1t[:, c, b * BLK:(b + 1) * BLK],
                        start=(c == 0), stop=(c == NCH - 1),
                        skip_group_check=not (c == 0 or c == NCH - 1)), deps=deps)
                q_state["last_qmm"] = last
                qe = S(lambda e, oc=oc, qs=qs, bnk=bnk: e.activation(
                    qbl[:, qs, oc, :], bnk[:, 0:BLK], AF.Identity,
                    bias=qb_sb[:, oc:oc + 1]), deps=[last, smalls_y])
                q_ev.append(qe)
                q_state["bw"][oc % 2] = qe
            q_evs[b] = q_ev

        for b in range(min(NB, NQ)):
            emit_q(b)
        wt_ld = None

        pair_rdy = [None] * NCH
        kvf_ready = None
        a5_prev = None
        rw_war = None
        for h in range(H):
            bnk = pE if h < 8 else pF
            c0 = h * HP
            ra1 = V(lambda e, c0=c0: e.tensor_scalar(
                rscr66[32:33, :], ccs[32:33, c0:c0 + HD],
                ccs[32:33, c0 + HD + 1:c0 + HD + 2], None, ALU.subtract),
                deps=[dsv65, smalls_y, rw_war])
            ra2 = V(lambda e: e.tensor_tensor(
                rscr66[32:33, :], rscr66[32:33, :], vng66_sb[32:33, :], ALU.mult),
                deps=[ra1])
            ra3 = V(lambda e, c0=c0: e.tensor_scalar(
                rscr66b[32:33, :], c4vb66[32:33, :],
                ccs[32:33, c0 + HD:c0 + HD + 1], None, ALU.mult),
                deps=[dsv65, v_c4vb, rw_war])
            ra4 = V(lambda e, h=h: e.tensor_tensor(
                rows66[32:33, h * HD:(h + 1) * HD], rscr66[32:33, :],
                rscr66b[32:33, :], ALU.add), deps=[ra2, ra3] + ms)
            rb1 = V(lambda e, c0=c0: e.tensor_scalar(
                rscr66[0:1, :], ccs[0:1, c0:c0 + HD],
                ccs[0:1, c0 + HD + 1:c0 + HD + 2], None, ALU.subtract),
                deps=[dsv64, rw_war])
            rb2 = V(lambda e: e.tensor_tensor(
                rscr66[0:1, :], rscr66[0:1, :], vng66_sb[0:1, :], ALU.mult),
                deps=[rb1])
            rb3 = V(lambda e, h=h: e.tensor_tensor(
                rows66[0:1, h * HD:(h + 1) * HD], rscr66[0:1, :],
                c4vb66[0:1, :], ALU.add),
                deps=[rb2, v_c4vb] + ms)
            rw_war = rb3
            rmm = T(lambda e, h=h, bnk=bnk: e.matmul(
                bnk[0:HD, (h % 8) * 64:(h % 8) * 64 + 64],
                kn2_sb[:], rows66[:, h * HD:(h + 1) * HD],
                start=True, stop=True),
                deps=[ra4, rb3, ev1, ev2, smalls_g, v_kn2n])
            a1 = V(lambda e, c0=c0: e.tensor_tensor(
                e1[:], kvsum[0:HD, c0:c0 + HD], G2_sb[:], ALU.mult),
                deps=[dkv, v_g2, kvf_ready])
            a2 = V(lambda e, c0=c0: e.tensor_scalar(
                e2[:], G2_sb[:], kvsum[0:HD, c0 + HD + 1:c0 + HD + 2], None,
                ALU.mult), deps=[dkv, v_g2])
            a3 = V(lambda e: e.tensor_tensor(e1[:], e1[:], e2[:], ALU.subtract),
                   deps=[a1, a2])
            a4 = V(lambda e, c0=c0: e.tensor_scalar(
                e2[:], T1_sb[:], kvsum[0:HD, c0 + HD:c0 + HD + 1], None,
                ALU.mult), deps=[v_t1, a3])
            a5 = V(lambda e: e.tensor_tensor(e1[:], e1[:], e2[:], ALU.add),
                   deps=[a4])
            a6 = V(lambda e, h=h, bnk=bnk: e.tensor_tensor(
                e1[:], e1[:], bnk[0:HD, (h % 8) * 64:(h % 8) * 64 + 64], ALU.add),
                deps=[a5, rmm])
            kvf_ready = V(lambda e, h=h: e.scalar_tensor_tensor(
                kvf_all[:, h * HD:(h + 1) * HD],
                e1[:], tempq[:, h:h + 1], ab_sb[:, h * HD:(h + 1) * HD],
                ALU.mult, ALU.add), deps=[a6, v_tempq] + ms)
            if h % 2 == 0:
                a5_prev = kvf_ready
            else:
                j = h // 2
                YK(lambda e, j=j: e.dma_start(
                    out=kvf2[0:HD, j * 128:j * 128 + HD],
                    in_=kvf_all[:, (2 * j) * HD:(2 * j + 1) * HD]),
                    deps=[a5_prev] + ms)
                pair_rdy[j] = YK(lambda e, j=j: e.dma_start(
                    out=kvf2[HD:128, j * 128 + HD:(j + 1) * 128],
                    in_=kvf_all[:, (2 * j + 1) * HD:(2 * j + 2) * HD]),
                    deps=[kvf_ready])

        attp_war = None
        hb_war = None
        xr2_war = xr_war
        zb_war = None
        yb_war = None
        ye_war = None
        xpb_war = [None, xr_war[1], None]
        xv_war = None
        pE_war = None
        x2b_war = [x1b_war, None, None]
        xob_war = None
        pT2_war = pT_war
        psum_war = {"pC": None, "pD": None, "pF": None, "pG": ev3}
        p2_banks = {0: ("pG", pG, "pC", pC), 1: ("pD", pD, "pF", pF)}
        pend = []
        p1_ev = []

        def emit_p2z_T(i, att_ev_l, p1_ev_l):
            nonlocal hb_war
            t2 = i % TPB
            par = i % 3
            n0, b0, n1, b1_ = p2_banks[i % 2]
            ldx2 = YX(lambda e, i=i: e.dma_start(
                out=xr[:, 0, :], in_=x_in[i * 128:(i + 1) * 128, :]),
                deps=[xr2_war[0]])
            zdone = []
            for sl, (nm, bnk) in ((0, (n0, b0)), (1, (n1, b1_))):
                deps = [p1_ev_l[-1], p2_ld, smalls_g, psum_war[nm]]
                if nm in ("pC", "pD"):
                    deps.append(att_ev_l[-1])
                last = None
                for c in range(NCH):
                    last = T(lambda e, sl=sl, c=c, t2=t2, bnk=bnk: e.matmul(
                        bnk[:], hb[:, c, t2 * 128:(t2 + 1) * 128],
                        wa[:, c, C + sl * 512:C + (sl + 1) * 512],
                        start=(c == 0), stop=False,
                        skip_group_check=(c != 0)), deps=deps)
                for cc in range(4):
                    c0 = sl * 4 + cc
                    last = T(lambda e, sl=sl, cc=cc, c0=c0, i=i, bnk=bnk: e.matmul(
                        bnk[:, cc * 128:(cc + 1) * 128],
                        x1t[:, c0, i * 128:(i + 1) * 128], eye_sb[:],
                        start=False, stop=False, skip_group_check=True))
                last = T(lambda e, sl=sl, bnk=bnk: e.matmul(
                    bnk[:], epad_sb[0:1, :],
                    p2b_sb[0:1, sl * 512:(sl + 1) * 512],
                    start=False, stop=True))
                zdone.append(last)
            hb_war = zdone[1]
            return (i, par, n0, b0, n1, b1_, ldx2, zdone)

        def emit_p2z_VS(st):
            nonlocal zb_war, last_sqrt
            i, par, n0, b0, n1, b1_, ldx2, zdone = st
            z0 = S(lambda e, b0=b0: e.copy(zb[:, 0:512], b0[:]),
                   deps=[zdone[0], zb_war])
            z1 = S(lambda e, b1_=b1_: e.copy(zb[:, 512:1024], b1_[:]),
                   deps=[zdone[1]])
            psum_war[n0] = z0
            psum_war[n1] = z1
            zb_war = z1
            c1 = V(lambda e: e.bn_stats(bnst[:, 0:6], zb[:, 0:512]),
                   deps=[z0, z1, last_sqrt])
            c2 = V(lambda e: e.bn_stats(bnst[:, 6:12], zb[:, 512:1024]), deps=[c1])
            c3 = V(lambda e: e.bn_aggr(mv2[:], bnst[:]), deps=[c1, c2])
            c4 = S(lambda e: e.activation(sd_c[:], mv2[:, 1:2], AF.Sqrt,
                                          bias=eps_c[:]), deps=[c3])
            last_sqrt = c4
            c5 = V(lambda e: e.reciprocal(rc_c[:], sd_c[:]), deps=[c4])
            c6 = V(lambda e: e.tensor_tensor(mrc_c[:], mv2[:, 0:1], rc_c[:],
                                             ALU.mult), deps=[c5])
            c7 = V(lambda e, i=i, par=par: e.tensor_scalar(
                xpb[:, par, :], xr[:, 0, :], mrc_c[:], None, ALU.subtract),
                deps=[c6, xpb_war[par], ldx2])
            xr2_war[0] = c7
            c8 = V(lambda e, par=par: e.scalar_tensor_tensor(
                xpb[:, par, :], zb[:], rc_c[:], xpb[:, par, :], ALU.mult, ALU.add),
                deps=[c7])
            d1_ = V(lambda e, par=par: e.bn_stats(bnst[:, 0:6], xpb[:, par, 0:512]),
                    deps=[c8, last_sqrt])
            d2_ = V(lambda e, par=par: e.bn_stats(bnst[:, 6:12],
                                                  xpb[:, par, 512:1024]), deps=[d1_])
            d3_ = V(lambda e: e.bn_aggr(mv2[:], bnst[:]), deps=[d1_, d2_])
            d4_ = S(lambda e: e.activation(sd_c[:], mv2[:, 1:2], AF.Sqrt,
                                           bias=eps_c[:]), deps=[d3_])
            last_sqrt = d4_
            d5_ = V(lambda e: e.reciprocal(rc_c[:], sd_c[:]), deps=[d4_])
            d6_ = V(lambda e: e.tensor_scalar(mrc_c[:], mv2[:, 0:1], rc_c[:],
                                              -1.0, ALU.mult, ALU.mult), deps=[d5_])
            d7_ = V(lambda e, par=par: e.tensor_scalar(
                x2b_t[:, par, :], xpb[:, par, :], rc_c[:], mrc_c[:],
                ALU.mult, ALU.add), deps=[d6_, x2b_war[par]])
            return (i, par, d7_)

        def emit_tail_head(pend):
            nonlocal pT2_war
            i, par, d7v = pend
            trl = None
            for c in range(NCH):
                trl = T(lambda e, c=c, par=par: e.transpose(
                    pT[:, c * 128:(c + 1) * 128],
                    x2b_t[:, par, c * 128:(c + 1) * 128], eye_sb[:]),
                    deps=[d7v, pT2_war])
            x2b_war[par] = trl
            evt = S(lambda e, i=i: e.copy(
                x1t[:, :, i * 128:(i + 1) * 128],
                pT[:].rearrange("p (c n) -> p c n", n=128)), deps=[trl])
            pT2_war = evt
            return evt

        def emit_tail_xv(pend, evt, p1_ev_l):
            nonlocal pE_war, xv_war
            i, par, d7v = pend
            xvl = None
            for c in range(NCH):
                xvl = T(lambda e, c=c, i=i: e.matmul(
                    pE[0:32, 0:128], vr_sb[:, c, :],
                    x1t[:, c, i * 128:(i + 1) * 128],
                    start=(c == 0), stop=(c == NCH - 1),
                    skip_group_check=not (c == 0 or c == NCH - 1)),
                    deps=[evt, smalls_g, p1_ev_l[-2], p1_ev_l[-1], pE_war,
                          kvf_ready])
            xvev = S(lambda e: e.copy(xv_sb[0:32, :], pE[0:32, 0:128]),
                     deps=[xvl, xv_war] + ms)
            pE_war = xvev
            return xvev

        def emit_tail_rest(pend, evt, xvev, p1_ev_l):
            nonlocal xv_war, ye_war, yb_war, xob_war, last_sqrt
            i, par, d7v = pend
            ydone = []
            for sl, bnk in ((0, pA), (1, pB)):
                deps = [evt, wt_ld, ye_war, q_evs[max(q_evs)][-1]]
                last = None
                for c in range(NCH):
                    last = T(lambda e, sl=sl, c=c, i=i, bnk=bnk: e.matmul(
                        bnk[:], x1t[:, c, i * 128:(i + 1) * 128],
                        wt_sb[:, c, sl * 512:(sl + 1) * 512],
                        start=(c == 0), stop=False,
                        skip_group_check=(c != 0)), deps=deps)
                last = T(lambda e, sl=sl, bnk=bnk: e.matmul(
                    bnk[:], xv_sb[:], ct_sb[:, sl * 512:(sl + 1) * 512],
                    start=False, stop=False, skip_group_check=True),
                    deps=[xvev])
                last = T(lambda e, sl=sl, bnk=bnk: e.matmul(
                    bnk[:], epad_sb[0:1, :],
                    fb_sb[0:1, sl * 512:(sl + 1) * 512],
                    start=False, stop=True))
                ydone.append(last)
            xv_war = ydone[1]
            ye0 = S(lambda e: e.copy(yb[:, 0:512], pA[:]), deps=[ydone[0], yb_war])
            ye1 = S(lambda e: e.copy(yb[:, 512:1024], pB[:]), deps=[ydone[1]])
            ye_war = ye1
            q_state["bw"][0] = ye0
            q_state["bw"][1] = ye1
            f1 = V(lambda e: e.bn_stats(bnst[:, 0:6], yb[:, 0:512]),
                   deps=[ye0, last_sqrt])
            f2 = V(lambda e: e.bn_stats(bnst[:, 6:12], yb[:, 512:1024]),
                   deps=[ye1, f1])
            f3 = V(lambda e: e.bn_aggr(mv2[:], bnst[:]), deps=[f1, f2])
            f4 = S(lambda e: e.activation(sd_c[:], mv2[:, 1:2], AF.Sqrt,
                                          bias=eps_c[:]), deps=[f3])
            last_sqrt = f4
            f5 = V(lambda e: e.reciprocal(rc_c[:], sd_c[:]), deps=[f4])
            f6 = V(lambda e: e.tensor_tensor(mrc_c[:], mv2[:, 0:1], rc_c[:],
                                             ALU.mult), deps=[f5])
            f7 = V(lambda e, par=par: e.tensor_scalar(
                xob[:], xpb[:, par, :], mrc_c[:], None, ALU.subtract),
                deps=[f6, xob_war])
            xpb_war[par] = f7
            f8 = V(lambda e: e.scalar_tensor_tensor(
                xob[:], yb[:], rc_c[:], xob[:], ALU.mult, ALU.add), deps=[f7])
            yb_war = f8
            st = YO(lambda e, i=i: e.dma_start(
                out=out_d[i * 128:(i + 1) * 128, :], in_=xob[:]), deps=[f8])
            xob_war = st

        for b in range(NB):
            qs = b % NQ
            att_ev = []
            am = None
            for j in range(NCH):
                bnk = pC if j % 2 == 0 else pD
                deps = [pair_rdy[j], q_evs[b][-1], smalls_y,
                        psum_war["pC" if j % 2 == 0 else "pD"],
                        pbank_war[2 + (j % 2)]]
                if len(att_ev) >= 2:
                    deps.append(att_ev[-2])
                T(lambda e, j=j, qs=qs, bnk=bnk: e.matmul(
                    bnk[:, 0:BLK], kvf2[:, j * 128:(j + 1) * 128],
                    qbl[:, qs, j, :], start=True, stop=False), deps=deps)
                am = T(lambda e, j=j, b=b, bnk=bnk: e.matmul(
                    bnk[:, 0:BLK], eye_sb[:], x1t[:, j, b * BLK:(b + 1) * BLK],
                    start=False, stop=True), deps=[])
                att_ev.append(S(lambda e, j=j, bnk=bnk: e.copy(
                    attp[:, j, :], bnk[:, 0:BLK]), deps=[am, attp_war]))
            if b + NQ < NB:
                emit_q(b + NQ, extra=[am])
            if wt_ld is None and (b + NQ >= NB - 1):
                wt_ld = GD(lambda e: e.dma_start(
                    out=wt_sb[:], in_=wT.rearrange("(c p) n -> p c n", p=128)),
                    deps=[q_state["last_qmm"]])
            p1_ev = []
            last_p1mm = None
            for oc in range(NCH):
                bnk = pE if oc % 2 == 0 else pF
                deps = [p1_ld, pE_war,
                        psum_war["pF"] if oc % 2 == 1 else None]
                if len(p1_ev) >= 2:
                    deps.append(p1_ev[-2])
                last = None
                for c in range(NCH):
                    last = T(lambda e, oc=oc, c=c, bnk=bnk: e.matmul(
                        bnk[:, 0:BLK], wa[:, c, oc * 128:(oc + 1) * 128],
                        attp[:, c, :], start=(c == 0), stop=(c == NCH - 1),
                        skip_group_check=not (c == 0 or c == NCH - 1)),
                        deps=deps + [att_ev[c]])
                last_p1mm = last
                p1_ev.append(S(lambda e, oc=oc, bnk=bnk: e.activation(
                    hb[:, oc, :], bnk[:, 0:BLK], AF.Gelu,
                    bias=p1b_sb[:, oc:oc + 1]), deps=[last, hb_war, smalls_y]))
            attp_war = last_p1mm

            for t2 in range(TPB):
                i = b * TPB + t2
                if len(pend) >= 2:
                    old = pend.pop(0)
                    evt = emit_tail_head(old)
                else:
                    old = None
                st = emit_p2z_T(i, att_ev, p1_ev)
                if old is not None:
                    xvev = emit_tail_xv(old, evt, p1_ev)
                pend.append(emit_p2z_VS(st))
                if old is not None:
                    emit_tail_rest(old, evt, xvev, p1_ev)
        while pend:
            old = pend.pop(0)
            evt = emit_tail_head(old)
            xvev = emit_tail_xv(old, evt, p1_ev)
            emit_tail_rest(old, evt, xvev, p1_ev)

        for s in ("sY", "sYx", "sYo", "sYk"):
            P.wait("y", s, P.val(s))
        P.wait("g", "sGD", P.val("sGD"))

        with nc.Block() as block:
            @block.tensor
            def _(e):
                for f in P.q["t"]:
                    f(e)

            @block.vector
            def _(e):
                for f in P.q["v"]:
                    f(e)

            @block.scalar
            def _(e):
                for f in P.q["s"]:
                    f(e)

            @block.gpsimd
            def _(e):
                for f in P.q["g"]:
                    f(e)

            @block.sync
            def _(e):
                for f in P.q["y"]:
                    f(e)

    return nc


def make_inputs(core, x, qkv_w, qkv_b, kn_g, kn_b, vn_g, vn_b, temperature,
                attn_bias, p1_w, p1_b, p2_w, p2_b, W, b, V_r, Cm, TL):
    f32 = np.float32
    batch = core // 2
    half = core % 2
    xs = np.ascontiguousarray(x[batch, half * TL:(half + 1) * TL, :], dtype=f32)

    def rowpad(vec, width):
        a = np.zeros((128, width), dtype=f32)
        a[0, :] = vec
        return a

    def _rows66(vec):
        a = np.zeros((33, 64), dtype=f32)
        a[0, :] = vec
        a[32, :] = vec
        return a

    def _kn2(row64, row65):
        a = np.zeros((128, 64), dtype=f32)
        a[0, :] = row64
        a[32, :] = row65
        return a

    return {
        "x_in": xs,
        "wkvT": np.ascontiguousarray(qkv_w[1024:3072].T, dtype=f32),
        "wqT": np.ascontiguousarray(qkv_w[0:1024].T, dtype=f32),
        "p1T": np.ascontiguousarray(p1_w.T, dtype=f32),
        "p2T": np.ascontiguousarray(p2_w.T, dtype=f32),
        "wT": np.ascontiguousarray(W.T, dtype=f32),
        "vrW": np.ascontiguousarray(V_r, dtype=f32),
        "ctp": np.concatenate([np.ascontiguousarray(Cm.T, dtype=f32),
                               np.zeros((96, 1024), dtype=f32)], axis=0),
        "qb_pp": np.ascontiguousarray(qkv_b[0:1024].reshape(8, 128).T, dtype=f32),
        "p1b_pp": np.ascontiguousarray(p1_b.reshape(8, 128).T, dtype=f32),
        "kvb_pad": np.ascontiguousarray(qkv_b[1024:3072].reshape(1, 2048), dtype=f32),
        "p2b_pad": np.ascontiguousarray(p2_b.reshape(1, 1024), dtype=f32),
        "fb_pad": np.ascontiguousarray(b.reshape(1, 1024), dtype=f32),

        "kng_pad": rowpad(kn_g, 64),
        "vng_pad": rowpad(vn_g, 64),
        "vnb_pad": rowpad(vn_b, 64),
        "vng66": _rows66(vn_g),
        "vnb66": _rows66(vn_b),
        "kn2_66": _kn2(kn_b, kn_g),
        "tempb": np.ascontiguousarray(
            np.broadcast_to(np.asarray(temperature, dtype=f32).reshape(1, H),
                            (HD, H))),
        "abt": np.ascontiguousarray(
            np.asarray(attn_bias, dtype=f32).transpose(1, 0, 2).reshape(HD, H * HD)),
        "eye_d": np.eye(128, dtype=BFNP),
        "epad_d": rowpad(np.ones(128, dtype=f32), 128).astype(BFNP),
    }


def kernel(L0_attn_x, L0_attn_qkv_w, L0_attn_qkv_b, L0_attn_kn_g, L0_attn_kn_b,
           L0_attn_vn_g, L0_attn_vn_b, L0_attn_temperature, L0_attn_attn_bias,
           L0_attn_p1_w, L0_attn_p1_b, L0_attn_p2_w, L0_attn_p2_b,
           L0_ffn_W, L0_ffn_b, L0_ffn_V_r, L0_ffn_C):
    NT = 16
    TL = NT * 128
    x = np.asarray(L0_attn_x, dtype=np.float32)
    args = [np.asarray(a, dtype=np.float32) for a in
            (L0_attn_qkv_w, L0_attn_qkv_b, L0_attn_kn_g, L0_attn_kn_b,
             L0_attn_vn_g, L0_attn_vn_b, L0_attn_temperature, L0_attn_attn_bias,
             L0_attn_p1_w, L0_attn_p1_b, L0_attn_p2_w, L0_attn_p2_b,
             L0_ffn_W, L0_ffn_b, L0_ffn_V_r, L0_ffn_C)]
    nc = build_nc(NT, QTOT=2 * TL)
    in_maps = [make_inputs(c, x, *args, TL=TL) for c in range(NCORE)]
    res = run_bass_kernel_spmd(nc, in_maps, core_ids=list(range(NCORE)))
    out = np.empty((4, 4096, 1024), dtype=np.float32)
    for c in range(NCORE):
        batch, half = c // 2, c % 2
        out[batch, half * TL:(half + 1) * TL, :] = np.asarray(res.results[c]["out"])
    return out
